# version 14 (final)
# speedup vs baseline: 1.1258x; 1.0065x over previous
import numpy as np
import concourse.bass as bass
import concourse.mybir as mybir
from concourse.bass_utils import run_bass_kernel_spmd

F32 = mybir.dt.float32
BF16 = mybir.dt.bfloat16
I32 = mybir.dt.int32
AF = mybir.ActivationFunctionType
ALU = mybir.AluOpType
AX = mybir.AxisListType

D_MODEL = 1024
D_INNER = 2048
D_STATE = 16
D_CONV = 4
DT_RANK = 64
BATCH = 4
SEQ = 4096
N_CORES = 8
WTOK = 3328
WOFF = 768
NWIN = 5
COUT = NWIN * 512
DOUT = 2048

NDMASLOT = 12


class _Q:
    def __init__(self, name, eng, sem_c, sem_d_list):
        self.name = name
        self.eng = eng
        self.semc = sem_c
        self.semd = sem_d_list
        self.ccount = 0
        self.dcount = 0
        self.waited = {}


class Emitter:

    def __init__(self, nc, ctx):
        self.nc = nc
        self.writers = {}
        self.readers = {}
        self.qs = {}
        for name, eng in [("pe", nc.tensor), ("act", nc.scalar), ("dve", nc.vector),
                          ("pool", nc.gpsimd), ("sp", nc.sync)]:
            sc = ctx.enter_context(nc.semaphore(f"semc_{name}"))
            sds = [ctx.enter_context(nc.semaphore(f"semd_{name}{i}")) for i in range(NDMASLOT)]
            self.qs[name] = _Q(name, eng, sc, sds)

    def _wait_spec(self, q, spec, force_same=False):
        if spec[0] == "c":
            _, pq_name, count = spec
            if pq_name == q.name and q.name == "pe" and not force_same:
                return
            key = (pq_name, "c")
            if q.waited.get(key, -1) >= count:
                return
            q.eng.wait_ge(self.qs[pq_name].semc, count)
            q.waited[key] = count
        else:
            _, pq_name, j = spec
            slot = j % NDMASLOT
            count = 16 * (j // NDMASLOT + 1)
            key = (pq_name, "d", slot)
            if q.waited.get(key, -1) >= count:
                return
            q.eng.wait_ge(self.qs[pq_name].semd[slot], count)
            q.waited[key] = count

    def op(self, qname, fn, reads=(), writes=(), scalar_reads=(), dma=False):
        q = self.qs[qname]
        if getattr(self, "_fence", None) is not None and self._fence_seen.get(qname) is not self._fence:
            for spec in self._fence:
                self._wait_spec(q, spec)
            self._fence_seen[qname] = self._fence
        for k in reads:
            w = self.writers.get(k)
            if w is not None:
                self._wait_spec(q, w)
        for k in scalar_reads:
            w = self.writers.get(k)
            if w is not None:
                self._wait_spec(q, w, force_same=True)
        for k in writes:
            w = self.writers.get(k)
            if w is not None:
                self._wait_spec(q, w)
            for rtok, rspec in self.readers.get(k, {}).items():
                self._wait_spec(q, rspec)
        if dma:
            j = q.dcount
            if j >= NDMASLOT:
                self._wait_spec(q, ("d", qname, j - NDMASLOT), force_same=True)
            inst = fn(q.eng)
            inst.then_inc(q.semd[j % NDMASLOT], 16)
            q.dcount += 1
            tok = ("d", qname, j)
        else:
            inst = fn(q.eng)
            inst.then_inc(q.semc, 1)
            q.ccount += 1
            tok = ("c", qname, q.ccount)
        for k in list(reads) + list(scalar_reads):
            self.readers.setdefault(k, {})[tok if tok[0] == "d" else tok[:2]] = tok
        for k in writes:
            self.writers[k] = tok
            self.readers[k] = {}
        return inst

    def fence(self):
        specs = []
        for pq in self.qs.values():
            if pq.ccount > 0:
                specs.append(("c", pq.name, pq.ccount))
            for j in range(max(0, pq.dcount - NDMASLOT), pq.dcount):
                specs.append(("d", pq.name, j))
        self._fence = tuple(specs)
        self._fence_seen = {}

    def drain(self):
        q = self.qs["pool"]
        for pq in self.qs.values():
            if pq.ccount > 0 and pq.name != q.name:
                self._wait_spec(q, ("c", pq.name, pq.ccount))
            for j in range(max(0, pq.dcount - NDMASLOT), pq.dcount):
                self._wait_spec(q, ("d", pq.name, j))


def _cp(e, dst, src):
    if hasattr(e, "tensor_copy"):
        return e.tensor_copy(dst, src)
    return e.copy(dst, src)


def emit_fp4(em, nc, src, src_keys, dst, dst_key, F, scr):
    if isinstance(src_keys, str):
        src_keys = [src_keys]
    tA, tB, tI, bm, r12, s12, gsc = scr
    NB = F // 16
    em.op("dve", lambda e: e.tensor_reduce(bm[:, :NB, None], src.rearrange("p (b k) -> p b k", k=16),
                                           AX.X, ALU.max, apply_absolute_value=True),
          reads=src_keys, writes=["q_bm"])
    em.op("dve", lambda e: e.tensor_scalar_max(bm[:, :NB], bm[:, :NB], 6e-12),
          reads=["q_bm"], writes=["q_bm"])
    em.op("dve", lambda e: e.reciprocal(r12[:, :NB], bm[:, :NB]), reads=["q_bm"], writes=["q_r"])
    em.op("dve", lambda e: e.tensor_scalar_mul(r12[:, :NB], r12[:, :NB], 12.0),
          reads=["q_r"], writes=["q_r"])
    em.op("dve", lambda e: e.tensor_scalar_mul(s12[:, :NB], bm[:, :NB], 1.0 / 12.0),
          reads=["q_bm"], writes=["q_s"])
    b3 = lambda t: t[:, 0:F].rearrange("p (b k) -> p b k", k=16)
    bc3 = lambda t: t[:, :NB, None].broadcast_to([128, NB, 16])
    em.op("dve", lambda e: e.tensor_tensor(b3(tB), b3(src), bc3(r12), ALU.mult),
          reads=src_keys + ["q_r"], writes=["q_tB"])
    em.op("dve", lambda e: e.tensor_scalar(gsc[:, 0:F], tB[:, 0:F], 0.0, -0.5, ALU.is_gt, ALU.add),
          reads=["q_tB"], writes=["q_g"])
    em.op("dve", lambda e: e.tensor_mul(tA[:, 0:F], tB[:, 0:F], gsc[:, 0:F]),
          reads=["q_tB", "q_g"], writes=["q_tA"])
    em.op("dve", lambda e: e.scalar_tensor_tensor(tB[:, 0:F], tA[:, 0:F], 2.0, gsc[:, 0:F],
                                                  ALU.min, ALU.mult),
          reads=["q_tA", "q_g"], writes=["q_tB"])
    em.op("dve", lambda e: e.tensor_scalar(tA[:, 0:F], tA[:, 0:F], -2.0, 0.0, ALU.add, ALU.max),
          reads=["q_tA"], writes=["q_tA"])
    em.op("dve", lambda e: e.tensor_scalar_min(tI[:, 0:F], tA[:, 0:F], 2.0),
          reads=["q_tA"], writes=["q_tI"])
    em.op("dve", lambda e: e.tensor_scalar(tA[:, 0:F], tA[:, 0:F], 3.0, 2.0, ALU.is_gt, ALU.mult),
          reads=["q_tA"], writes=["q_tA"])
    em.op("dve", lambda e: e.tensor_tensor(tA[:, 0:F], tI[:, 0:F], tA[:, 0:F], ALU.add),
          reads=["q_tI", "q_tA"], writes=["q_tA"])
    em.op("dve", lambda e: e.tensor_mul(tA[:, 0:F], tA[:, 0:F], gsc[:, 0:F]),
          reads=["q_g", "q_tA"], writes=["q_tA"])
    em.op("dve", lambda e: e.tensor_scalar_mul(tI[:, 0:F], tB[:, 0:F], 4.0),
          reads=["q_tB"], writes=["q_tI"])
    em.op("dve", lambda e: e.scalar_tensor_tensor(tA[:, 0:F], tA[:, 0:F], 4.0, tI[:, 0:F],
                                                  ALU.mult, ALU.add),
          reads=["q_tI", "q_tA"], writes=["q_tA"])
    em.op("dve", lambda e: e.tensor_tensor(b3(dst), b3(tA), bc3(s12), ALU.mult),
          reads=["q_tA", "q_s"], writes=[dst_key])


def emit_ln(em, nc, xr, xr_keys, sq, sq_key, mv, epsb, tag):
    em.op("dve", lambda e: e.tensor_reduce(mv[:, 0:1], xr[:, 0:D_MODEL], AX.X, ALU.add),
          reads=xr_keys, writes=[tag + "mv0"])
    em.op("act", lambda e: e.activation(sq[:, 0:D_MODEL], xr[:, 0:D_MODEL], AF.Square,
                                        accum_out=mv[:, 1:2]),
          reads=xr_keys, writes=[tag + "mv1", sq_key])
    em.op("dve", lambda e: e.tensor_scalar_mul(mv[:, 0:2], mv[:, 0:2], 1.0 / D_MODEL),
          reads=[tag + "mv0", tag + "mv1"], writes=[tag + "mv0", tag + "mv1"])
    em.op("dve", lambda e: e.scalar_tensor_tensor(
        mv[:, 2:3], mv[:, 0:1], -1.0, mv[:, 0:1], ALU.mult, ALU.mult),
        reads=[tag + "mv0"], writes=[tag + "mv2"])
    em.op("dve", lambda e: e.tensor_add(mv[:, 2:3], mv[:, 2:3], mv[:, 1:2]),
          reads=[tag + "mv1", tag + "mv2"], writes=[tag + "mv2"])
    em.op("act", lambda e: e.activation(mv[:, 2:3], mv[:, 2:3], AF.Ln, bias=epsb[:, 0:1]),
          reads=[tag + "mv2"], scalar_reads=["epsb"], writes=[tag + "mv2"])
    em.op("act", lambda e: e.activation(mv[:, 2:3], mv[:, 2:3], AF.Exp, scale=-0.5),
          reads=[tag + "mv2"], writes=[tag + "mv2"])
    em.op("dve", lambda e: e.tensor_scalar(
        xr[:, 0:D_MODEL], xr[:, 0:D_MODEL], mv[:, 0:1], mv[:, 2:3],
        ALU.subtract, ALU.mult),
        reads=xr_keys, scalar_reads=[tag + "mv0", tag + "mv2"], writes=xr_keys)


def build_fused():
    from contextlib import ExitStack
    nc = bass.Bass("TRN2", target_bir_lowering=False, debug=False)
    TH = SEQ + 3

    xT = nc.dram_tensor("xT", [D_MODEL, TH], F32, kind="ExternalInput").ap()
    xres_win = nc.dram_tensor("xres_win", [WTOK, D_MODEL], F32, kind="ExternalInput").ap()
    msk_in = nc.dram_tensor("msk", [128, 2], F32, kind="ExternalInput").ap()
    W_inT = nc.dram_tensor("W_inT", [D_MODEL, 2 * D_INNER], F32, kind="ExternalInput").ap()
    conv_w = nc.dram_tensor("conv_w", [128, 16 * D_CONV], F32, kind="ExternalInput").ap()
    conv_b = nc.dram_tensor("conv_b", [128, 16], F32, kind="ExternalInput").ap()
    W_xT = nc.dram_tensor("W_xT", [D_INNER, 96], F32, kind="ExternalInput").ap()
    W_dtT = nc.dram_tensor("W_dtT", [DT_RANK, D_INNER], F32, kind="ExternalInput").ap()
    b_dt = nc.dram_tensor("b_dt", [128, 16], F32, kind="ExternalInput").ap()
    alog_in = nc.dram_tensor("alog", [128, 16 * D_STATE], F32, kind="ExternalInput").ap()
    Dv_in = nc.dram_tensor("Dv", [128, 16], F32, kind="ExternalInput").ap()
    WoT = nc.dram_tensor("WoT", [D_INNER, D_MODEL], F32, kind="ExternalInput").ap()
    DH_B = 2816
    wg = nc.dram_tensor("wg", [DH_B, D_MODEL], F32, kind="ExternalInput").ap()
    wu = nc.dram_tensor("wu", [DH_B, D_MODEL], F32, kind="ExternalInput").ap()
    wdn = nc.dram_tensor("wdn", [D_MODEL, DH_B], F32, kind="ExternalInput").ap()
    wqkv = nc.dram_tensor("wqkv", [D_MODEL, 3 * D_MODEL], F32, kind="ExternalInput").ap()
    bqkv = nc.dram_tensor("bqkv", [128, 24], F32, kind="ExternalInput").ap()
    wout = nc.dram_tensor("wout", [D_MODEL, D_MODEL], F32, kind="ExternalInput").ap()
    bout = nc.dram_tensor("bout", [1, D_MODEL], F32, kind="ExternalInput").ap()
    DH_D = 4096
    w1T = nc.dram_tensor("w1T", [D_MODEL, DH_D], F32, kind="ExternalInput").ap()
    cw1 = nc.dram_tensor("cw1", [128, 32 * 3], F32, kind="ExternalInput").ap()
    cb1 = nc.dram_tensor("cb1", [128, 32], F32, kind="ExternalInput").ap()
    w2T = nc.dram_tensor("w2T", [DH_D, D_MODEL], F32, kind="ExternalInput").ap()

    d_dt = nc.dram_tensor("d_dt", [D_INNER, SEQ], F32, kind="Internal").ap()
    d_dtx = nc.dram_tensor("d_dtx", [D_INNER, SEQ], BF16, kind="Internal").ap()
    d_zs = nc.dram_tensor("d_zs", [D_INNER, SEQ], BF16, kind="Internal").ap()
    d_xc = nc.dram_tensor("d_xc", [D_INNER, SEQ], BF16, kind="Internal").ap()
    d_bc = nc.dram_tensor("d_bc", [32, SEQ], BF16, kind="Internal").ap()
    d_y = nc.dram_tensor("d_y", [D_INNER, SEQ], BF16, kind="Internal").ap()
    d_yf = nc.dram_tensor("d_yf", [D_INNER, WTOK], BF16, kind="Internal").ap()
    d_x1 = nc.dram_tensor("d_x1", [WTOK, D_MODEL], F32, kind="Internal").ap()
    d_x2 = nc.dram_tensor("d_x2", [WTOK, D_MODEL], F32, kind="Internal").ap()
    d_x2res = nc.dram_tensor("d_x2res", [COUT, D_MODEL], F32, kind="Internal").ap()
    d_x3tok = nc.dram_tensor("d_x3tok", [COUT, D_MODEL], F32, kind="Internal").ap()
    d_x3T = nc.dram_tensor("d_x3T", [D_MODEL, COUT + 2], BF16, kind="Internal").ap()
    d_x3w = nc.dram_tensor("d_x3w", [D_MODEL, DOUT + 2], BF16, kind="Internal").ap()
    d_x3res = nc.dram_tensor("d_x3res", [DOUT, D_MODEL], F32, kind="Internal").ap()

    o_out = nc.dram_tensor("out_tok", [DOUT, D_MODEL], F32, kind="ExternalOutput").ap()

    ctx = ExitStack()
    sb = lambda name, shape, dt: ctx.enter_context(nc.sbuf_tensor(name, shape, dt))

    msk = sb("mskt", [128, 2], F32)
    epsb = sb("epsb", [128, 1], F32)
    idb = sb("idb", [128, 128], BF16)
    zrowb = sb("zrowb", [128, 2], BF16)

    def blend(qn, out_ap, a_ap, b_ap, reads, wkey, tag="bl"):
        em.op(qn, lambda e: e.tensor_scalar(out_ap, a_ap, msk[:, 1:2], None, ALU.mult),
              reads=reads, scalar_reads=["mskt"], writes=[wkey])
        em.op(qn, lambda e: e.scalar_tensor_tensor(out_ap, b_ap, msk[:, 0:1], out_ap,
                                                   ALU.mult, ALU.add),
              reads=reads + [wkey], scalar_reads=["mskt"], writes=[wkey])

    with ctx, nc.Block() as block:
        em = Emitter(nc, ctx)

        em.op("sp", lambda e: e.dma_start(out=msk[:], in_=msk_in), writes=["mskt"], dma=True)
        em.op("pool", lambda e: e.memset(epsb[:], 1e-5), writes=["epsb"])
        em.op("pool", lambda e: e.memset(zrowb[:], 0.0), writes=["zrowb"])
        em.op("pool", lambda e: e.memset(idb[:], 0.0), writes=["idb"])
        em.op("pool", lambda e: e.affine_select(out=idb[:], in_=idb[:], compare_op=ALU.not_equal,
                                                fill=1.0, base=0, pattern=[[-1, 128]],
                                                channel_multiplier=1),
              reads=["idb"], writes=["idb"])
        for k in range(8):
            em.op("sp", lambda e, k=k: e.dma_start(
                out=d_x3T[k * 128:(k + 1) * 128, 0:1], in_=zrowb[:, 0:1],
                allow_slow_non_contiguous=True),
                reads=["zrowb"], dma=True)
            em.op("sp", lambda e, k=k: e.dma_start(
                out=d_x3T[k * 128:(k + 1) * 128, COUT + 1:COUT + 2], in_=zrowb[:, 0:1],
                allow_slow_non_contiguous=True),
                reads=["zrowb"], dma=True)

        with ExitStack() as p1:
            s1 = lambda n, sh, dt: p1.enter_context(nc.sbuf_tensor(n, sh, dt))
            xtb = [s1(f"xtb{k}", [128, TH], BF16) for k in range(8)]
            stage = s1("stage", [128, 2052], F32)
            wstg = [s1(f"wstg{i}", [128, 128], F32) for i in range(2)]
            wsl = [[s1(f"wsl{p}_{k}", [128, 128], BF16) for k in range(8)] for p in range(2)]
            wxb = s1("wxb", [128, 16 * 96], BF16)
            wdtb = s1("wdtb", [64, D_INNER], BF16)
            cwt = s1("cwt", [128, 16 * D_CONV], F32)
            cbt = s1("cbt", [128, 16], F32)
            bdt_t = s1("bdt", [128, 16], F32)
            xin = s1("xin", [128, TH], F32)
            xcv = s1("xcv", [128, SEQ], F32)
            xcb = s1("xcb", [128, SEQ], BF16)
            zstage = s1("zstage", [128, SEQ], BF16)
            xdbl = s1("xdbl", [96, SEQ], F32)
            dtrb = s1("dtrb", [64, SEQ], BF16)
            bcb1 = s1("bcb1", [32, SEQ], BF16)
            sp1 = s1("sp1", [128, 512], F32)
            sp2 = s1("sp2", [128, 512], F32)
            dtt1 = s1("dtt1", [128, 512], F32)
            dtxb = s1("dtxb", [128, 512], BF16)
            xcg = s1("xcg", [128, 512], BF16)

            em.op("sp", lambda e: e.dma_start(out=cwt[:], in_=conv_w), writes=["cwt"], dma=True)
            em.op("sp", lambda e: e.dma_start(out=cbt[:], in_=conv_b), writes=["cbt"], dma=True)
            em.op("sp", lambda e: e.dma_start(out=bdt_t[:], in_=b_dt), writes=["bdt"], dma=True)
            em.op("pool", lambda e: e.memset(xdbl[:], 0.0), writes=["xdbl"])
            for k in range(8):
                em.op("sp", lambda e, k=k: e.dma_start(out=stage[:, 0:2048], in_=xT[k * 128:(k + 1) * 128, 0:2048]),
                      writes=["stage"], dma=True)
                em.op("dve", lambda e, k=k: e.tensor_copy(xtb[k][:, 0:2048], stage[:, 0:2048]),
                      reads=["stage"], writes=[f"xtb{k}"])
                em.op("sp", lambda e, k=k: e.dma_start(out=stage[:, 0:TH - 2048], in_=xT[k * 128:(k + 1) * 128, 2048:TH]),
                      writes=["stage"], dma=True)
                em.op("dve", lambda e, k=k: e.tensor_copy(xtb[k][:, 2048:TH], stage[:, 0:TH - 2048]),
                      reads=["stage"], writes=[f"xtb{k}"])
            for c in range(16):
                em.op("sp", lambda e, c=c: e.dma_start(out=stage[:, 0:96], in_=W_xT[c * 128:(c + 1) * 128, :]),
                      writes=["stage"], dma=True)
                em.op("dve", lambda e, c=c: e.tensor_copy(wxb[:, c * 96:(c + 1) * 96], stage[:, 0:96]),
                      reads=["stage"], writes=["wxb"])
            em.op("sp", lambda e: e.dma_start(out=stage[:64, 0:D_INNER], in_=W_dtT),
                  writes=["stage"], dma=True)
            em.op("dve", lambda e: e.tensor_copy(wdtb[:], stage[:64, 0:D_INNER]),
                  reads=["stage"], writes=["wdtb"])

            with ExitStack() as pp1:
                pband = [pp1.enter_context(nc.psum_tensor(f"pband{i}", [128, 512], F32)) for i in range(4)]
                pxd = pp1.enter_context(nc.psum_tensor("pxd", [96, 512], F32))

                NG = [(g * 512, 512) for g in range(8)]
                for m in range(32):
                    inner = m < 16
                    par = m % 2
                    for k in range(8):
                        wb = k % 2
                        em.op("sp" if k % 2 == 0 else "pool",
                              lambda e, m=m, k=k, wb=wb: e.dma_start(
                                  out=wstg[wb][:], in_=W_inT[k * 128:(k + 1) * 128, m * 128:(m + 1) * 128]),
                              writes=[f"wstg{wb}"], dma=True)
                        em.op("dve" if k % 2 == 0 else "pool",
                              lambda e, k=k, par=par, wb=wb: e.tensor_copy(wsl[par][k][:], wstg[wb][:]),
                              reads=[f"wstg{wb}"], writes=[f"wsl{par}_{k}"])
                    groups = NG + [(4096, 3)] if inner else [(3 + g0, gl) for g0, gl in NG]
                    for gi, (g0, gl) in enumerate(groups):
                        pt = pband[gi % 4]
                        pkey = f"pband{gi % 4}"
                        for k in range(8):
                            em.op("pe", lambda e, k=k, g0=g0, gl=gl, pt=pt, par=par: e.matmul(
                                pt[:, :gl], wsl[par][k][:], xtb[k][:, g0:g0 + gl],
                                start=(k == 0), stop=(k == 7)),
                                reads=[f"wsl{par}_{k}", f"xtb{k}"],
                                writes=[pkey] if k in (0, 7) else [])
                        if inner:
                            em.op("act", lambda e, g0=g0, gl=gl, pt=pt: e.copy(xin[:, g0:g0 + gl], pt[:, :gl]),
                                  reads=[pkey], writes=["xin"])
                        else:
                            em.op("act", lambda e, g0=g0, gl=gl, pt=pt: e.activation(
                                zstage[:, g0 - 3:g0 - 3 + gl], pt[:, :gl], AF.Silu),
                                reads=[pkey], writes=["zstage"])
                    if inner:
                        em.op("dve", lambda e, m=m: e.tensor_scalar(
                            xcv[:], xin[:, 0:SEQ], cwt[:, m * 4:m * 4 + 1], None, ALU.mult),
                            reads=["xin"], scalar_reads=["cwt"], writes=["xcv"])
                        for k in range(1, 4):
                            em.op("dve", lambda e, m=m, k=k: e.scalar_tensor_tensor(
                                xcv[:], xin[:, k:k + SEQ], cwt[:, m * 4 + k:m * 4 + k + 1], xcv[:],
                                ALU.mult, ALU.add),
                                reads=["xin", "xcv"], scalar_reads=["cwt"], writes=["xcv"])
                        em.op("act", lambda e, m=m: e.activation(
                            xcb[:], xcv[:], AF.Silu, bias=cbt[:, m:m + 1]),
                            reads=["xcv"], scalar_reads=["cbt"], writes=["xcb"])
                        em.op("sp", lambda e, m=m: e.dma_start(
                            out=d_xc[m * 128:(m + 1) * 128, :], in_=xcb[:]),
                            reads=["xcb"], writes=[("dxc", m)], dma=True)
                        for g in range(8):
                            em.op("pe", lambda e, m=m, g=g: e.matmul(
                                pxd[:], wxb[:, m * 96:(m + 1) * 96], xcb[:, g * 512:(g + 1) * 512],
                                start=True, stop=True),
                                reads=["wxb", "xcb"], writes=["pxd"])
                            em.op("dve", lambda e, g=g: e.tensor_add(
                                xdbl[:, g * 512:(g + 1) * 512], xdbl[:, g * 512:(g + 1) * 512], pxd[:]),
                                reads=["pxd", "xdbl"], writes=["xdbl"])
                    else:
                        em.op("sp", lambda e, m=m: e.dma_start(
                            out=d_zs[(m - 16) * 128:(m - 15) * 128, :], in_=zstage[:]),
                            reads=["zstage"], dma=True)

                em.op("dve", lambda e: e.tensor_copy(dtrb[:], xdbl[0:64, :]),
                      reads=["xdbl"], writes=["dtrb"])
                em.op("dve", lambda e: e.tensor_copy(bcb1[:], xdbl[64:96, :]),
                      reads=["xdbl"], writes=["bcb1"])
                em.op("sp", lambda e: e.dma_start(out=d_bc, in_=bcb1[:]),
                      reads=["bcb1"], dma=True)

                for m in range(16):
                    for g in range(8):
                        em.op("pe", lambda e, m=m, g=g: e.matmul(
                            pband[g % 2][:], wdtb[:, m * 128:(m + 1) * 128],
                            dtrb[:, g * 512:(g + 1) * 512], start=True, stop=True),
                            reads=["wdtb", "dtrb"], writes=[f"pband{g % 2}"])
                        pk = f"pband{g % 2}"
                        em.op("act", lambda e, m=m, g=g: e.activation(
                            sp1[:], pband[g % 2][:], AF.Abs, bias=bdt_t[:, m:m + 1]),
                            reads=[pk], scalar_reads=["bdt"], writes=["sp1"])
                        em.op("act", lambda e: e.activation(sp2[:], sp1[:], AF.Exp, scale=-1.0),
                              reads=["sp1"], writes=["sp2"])
                        em.op("act", lambda e: e.activation(sp1[:], sp2[:], AF.Ln, bias=1.0),
                              reads=["sp2"], writes=["sp1"])
                        em.op("dve", lambda e, m=m, g=g: e.tensor_scalar(
                            sp2[:], pband[g % 2][:], bdt_t[:, m:m + 1], 0.0, ALU.add, ALU.max),
                            reads=[pk], scalar_reads=["bdt"], writes=["sp2"])
                        em.op("dve", lambda e: e.tensor_add(dtt1[:], sp1[:], sp2[:]),
                              reads=["sp1", "sp2"], writes=["dtt1"])
                        em.op("sp", lambda e, m=m, g=g: e.dma_start(
                            out=d_dt[m * 128:(m + 1) * 128, g * 512:(g + 1) * 512], in_=dtt1[:]),
                            reads=["dtt1"], dma=True)
                        em.op("pool", lambda e, m=m, g=g: e.dma_start(
                            out=xcg[:], in_=d_xc[m * 128:(m + 1) * 128, g * 512:(g + 1) * 512]),
                            reads=[("dxc", m)], writes=["xcg"], dma=True)
                        em.op("dve", lambda e: e.tensor_mul(dtxb[:], dtt1[:], xcg[:]),
                              reads=["dtt1", "xcg"], writes=["dtxb"])
                        em.op("pool", lambda e, m=m, g=g: e.dma_start(
                            out=d_dtx[m * 128:(m + 1) * 128, g * 512:(g + 1) * 512], in_=dtxb[:]),
                            reads=["dtxb"], dma=True)

        em.fence()
        TC = 512
        TCP = TC + 1
        with ExitStack() as p2:
            s2 = lambda n, sh, dt: p2.enter_context(nc.sbuf_tensor(n, sh, dt))
            at = s2("at", [128, 16 * D_STATE], F32)
            sel = s2("sel", [32, 32 * 128], BF16)
            bct = s2("bct", [32, TC], BF16)
            Bb = [s2("BbT0", [128, D_STATE, TC], BF16)]
            Cb = [s2("CbT0", [128, D_STATE, TC], BF16)]
            dtt = [s2(f"dttS{i}", [128, TC], F32) for i in range(3)]
            dtxt = [s2(f"dtxS{i}", [128, TC], BF16) for i in range(3)]
            dA = [s2(f"dA{i}", [128, D_STATE * TCP], F32) for i in range(2)]
            ut = [s2(f"utS{i}", [128, D_STATE * TCP], BF16) for i in range(2)]
            st = [s2(f"stP{i}", [128, D_STATE * TCP], BF16) for i in range(2)]
            stcar = s2("stcar", [128, 16 * D_STATE], F32)
            ycb = [s2(f"ycbP{i}", [128, TC], BF16) for i in range(2)]
            dA3 = [t.rearrange("p (n t) -> p n t", t=TCP) for t in dA]
            ut3 = [t.rearrange("p (n t) -> p n t", t=TCP) for t in ut]
            st3 = [t.rearrange("p (n t) -> p n t", t=TCP) for t in st]

            em.op("sp", lambda e: e.dma_start(out=at[:], in_=alog_in), writes=["at"], dma=True)
            em.op("act", lambda e: e.activation(at[:], at[:], AF.Exp), reads=["at"], writes=["at"])
            em.op("dve", lambda e: e.tensor_scalar_mul(at[:], at[:], -1.0), reads=["at"], writes=["at"])
            em.op("pool", lambda e: e.memset(sel[:], 1.0), writes=["sel"])
            for n in range(32):
                em.op("pool", lambda e, n=n: e.affine_select(
                    out=sel[:, n * 128:(n + 1) * 128], in_=sel[:, n * 128:(n + 1) * 128],
                    compare_op=ALU.is_equal, fill=0.0, base=-n, pattern=[[0, 128]],
                    channel_multiplier=1), reads=["sel"], writes=["sel"] if n == 31 else [])
            em.op("pool", lambda e: e.memset(stcar[:], 0.0), writes=["stcarA", "stcarB"])
            em.op("pool", lambda e: e.memset(dA[0][:], 0.0), writes=["dAz0"])
            em.op("pool", lambda e: e.memset(dA[1][:], 0.0), writes=["dAz1"])

            with ExitStack() as pp2:
                pbc = [pp2.enter_context(nc.psum_tensor(f"pbc{i}", [128, TC], F32)) for i in range(2)]
                for tc in range(8):
                    t0 = tc * TC
                    bu = 0
                    if tc > 0:
                        tail(tc - 1, 15)
                    em.op("sp", lambda e, t0=t0: e.dma_start(out=bct[:], in_=d_bc[:, t0:t0 + TC]),
                          writes=["bct"], dma=True)
                    for n in range(32):
                        dst = (Bb[bu] if n < 16 else Cb[bu])
                        dkey = (f"Bb{bu}" if n < 16 else f"Cb{bu}")
                        pi = n % 2
                        em.op("pe", lambda e, n=n, pi=pi: e.matmul(
                            pbc[pi][:], sel[:, n * 128:(n + 1) * 128], bct[:], start=True, stop=True),
                            reads=["sel", "bct"], writes=[f"pbc{pi}"])
                        em.op("act", lambda e, n=n, pi=pi, dst=dst: e.copy(
                            dst[:, n % 16, :], pbc[pi][:]),
                            reads=[f"pbc{pi}"], writes=[dkey])
                    def tail(tcq, dcq):
                        paq = dcq % 2
                        sb = dcq % 2
                        buq = 0
                        tq0 = tcq * TC
                        for qn, h0, nh, hl in (("dve", 0, 12, "A"), ("pool", 12, 4, "B")):
                            em.op(qn, lambda e, paq=paq, sb=sb, h0=h0, nh=nh, buq=buq: e.tensor_tensor(
                                ut3[paq][:, h0:h0 + nh, 1:TCP], st3[sb][:, h0:h0 + nh, 1:TCP],
                                Cb[buq][:, h0:h0 + nh, :], ALU.mult),
                                reads=[f"st{hl}{sb}", f"Cb{buq}"], writes=[f"ut{hl}{paq}"])
                        em.op("dve", lambda e, paq=paq: e.tensor_tensor(
                            ut3[paq][:, 0:8, 1:TCP], ut3[paq][:, 0:8, 1:TCP],
                            ut3[paq][:, 8:16, 1:TCP], ALU.add),
                            reads=[f"utA{paq}", f"utB{paq}"], writes=[f"utA{paq}"])
                        em.op("dve", lambda e, paq=paq: e.tensor_tensor(
                            ut3[paq][:, 0:4, 1:TCP], ut3[paq][:, 0:4, 1:TCP],
                            ut3[paq][:, 4:8, 1:TCP], ALU.add),
                            reads=[f"utA{paq}"], writes=[f"utA{paq}"])
                        em.op("dve", lambda e, paq=paq: e.tensor_tensor(
                            ut3[paq][:, 0:2, 1:TCP], ut3[paq][:, 0:2, 1:TCP],
                            ut3[paq][:, 2:4, 1:TCP], ALU.add),
                            reads=[f"utA{paq}"], writes=[f"utA{paq}"])
                        em.op("dve", lambda e, paq=paq, sb=sb: e.tensor_tensor(
                            ycb[sb][:], ut3[paq][:, 0, 1:TCP], ut3[paq][:, 1, 1:TCP], ALU.add),
                            reads=[f"utA{paq}"], writes=[f"ycb{sb}"])
                        em.op("sp", lambda e, tq0=tq0, dcq=dcq, sb=sb: e.dma_start(
                            out=d_y[dcq * 128:(dcq + 1) * 128, tq0:tq0 + TC], in_=ycb[sb][:]),
                            reads=[f"ycb{sb}"], dma=True)

                    for dc in range(16):
                        gi = tc * 16 + dc
                        db = gi % 3
                        pa = dc % 2
                        sb = dc % 2
                        gpre = gi + 2
                        if dc == 0 and tc == 0:
                            for gp in (0, 1, 2):
                                dbp = gp % 3
                                tp0 = (gp // 16) * TC
                                dcp = gp % 16
                                em.op("sp", lambda e, tp0=tp0, dcp=dcp, dbp=dbp: e.dma_start(
                                    out=dtt[dbp][:], in_=d_dt[dcp * 128:(dcp + 1) * 128, tp0:tp0 + TC]),
                                    writes=[f"dtt{dbp}"], dma=True)
                                em.op("sp", lambda e, tp0=tp0, dcp=dcp, dbp=dbp: e.dma_start(
                                    out=dtxt[dbp][:], in_=d_dtx[dcp * 128:(dcp + 1) * 128, tp0:tp0 + TC]),
                                    writes=[f"dtxt{dbp}"], dma=True)
                        elif gpre <= 127:
                            gp = gpre
                            dbp = gp % 3
                            tp0 = (gp // 16) * TC
                            dcp = gp % 16
                            em.op("sp", lambda e, tp0=tp0, dcp=dcp, dbp=dbp: e.dma_start(
                                out=dtt[dbp][:], in_=d_dt[dcp * 128:(dcp + 1) * 128, tp0:tp0 + TC]),
                                writes=[f"dtt{dbp}"], dma=True)
                            em.op("sp", lambda e, tp0=tp0, dcp=dcp, dbp=dbp: e.dma_start(
                                out=dtxt[dbp][:], in_=d_dtx[dcp * 128:(dcp + 1) * 128, tp0:tp0 + TC]),
                                writes=[f"dtxt{dbp}"], dma=True)
                        NA = 12
                        for n in range(D_STATE):
                            hl = "A" if n < NA else "B"
                            em.op("act", lambda e, n=n, dc=dc, db=db, pa=pa: e.activation(
                                dA3[pa][:, n, 1:TCP], dtt[db][:], AF.Exp,
                                scale=at[:, dc * 16 + n:dc * 16 + n + 1]),
                                reads=[f"dtt{db}", f"dAz{pa}"], scalar_reads=["at"],
                                writes=[f"dA{hl}{pa}"] if n in (0, NA - 1, NA, 15) else [])
                        for qn, h0, nh, hl in (("dve", 0, 12, "A"), ("pool", 12, 4, "B")):
                            em.op(qn, lambda e, db=db, pa=pa, h0=h0, nh=nh, bu=bu: e.tensor_tensor(
                                ut3[pa][:, h0:h0 + nh, 1:TCP],
                                dtxt[db][:, None, :].broadcast_to([128, nh, TC]),
                                Bb[bu][:, h0:h0 + nh, :], ALU.mult),
                                reads=[f"dtxt{db}", f"Bb{bu}"], writes=[f"ut{hl}{pa}"])
                            em.op(qn, lambda e, dc=dc, pa=pa, h0=h0, nh=nh: e.tensor_copy(
                                ut3[pa][:, h0:h0 + nh, 0], stcar[:, dc * 16 + h0:dc * 16 + h0 + nh]),
                                reads=[f"stcar{hl}"], writes=[f"ut{hl}{pa}"])
                            em.op("dve", lambda e, pa=pa, sb=sb, h0=h0, nh=nh: e.tensor_tensor_scan(
                                st[sb][:, h0 * TCP:(h0 + nh) * TCP],
                                dA[pa][:, h0 * TCP:(h0 + nh) * TCP],
                                ut[pa][:, h0 * TCP:(h0 + nh) * TCP], 0.0, ALU.mult, ALU.add),
                                reads=[f"dA{hl}{pa}", f"ut{hl}{pa}"], writes=[f"st{hl}{sb}"])
                            em.op(qn, lambda e, dc=dc, sb=sb, h0=h0, nh=nh: e.tensor_copy(
                                stcar[:, dc * 16 + h0:dc * 16 + h0 + nh], st3[sb][:, h0:h0 + nh, TC]),
                                reads=[f"st{hl}{sb}"], writes=[f"stcar{hl}"])
                        if dc > 0:
                            tail(tc, dc - 1)
                    if tc == 7:
                        tail(7, 15)

        em.fence()
        with ExitStack() as p3:
            s3 = lambda n, sh, dt: p3.enter_context(nc.sbuf_tensor(n, sh, dt))
            ya = [s3(f"ya{i}", [128, WTOK], BF16) for i in range(2)]
            yb = [s3(f"yb{i}", [128, WTOK], BF16) for i in range(2)]
            za = [s3(f"za{i}", [128, WTOK], BF16) for i in range(2)]
            zb = [s3(f"zb{i}", [128, WTOK], BF16) for i in range(2)]
            xa = [s3(f"xa{i}", [128, WTOK], BF16) for i in range(2)]
            xb2 = [s3(f"xb2_{i}", [128, WTOK], BF16) for i in range(2)]
            ybl = s3("ybl", [128, WTOK], BF16)
            zbl = s3("zbl", [128, WTOK], BF16)
            xbl = s3("xbl", [128, WTOK], BF16)
            yf = s3("yf", [128, WTOK], BF16)
            dvt = s3("dvt", [128, 16], F32)
            wob = [s3(f"wob{k}", [128, D_MODEL], BF16) for k in range(16)]
            stage3 = s3("stage3", [128, D_MODEL], F32)
            yfg = [s3(f"yfg{i}", [128, 384], BF16) for i in range(2)]
            xr3 = s3("xr3", [128, D_MODEL], F32)
            sq3 = s3("sq3", [128, D_MODEL], F32)
            mv3 = s3("mv3", [128, 4], F32)

            em.op("sp", lambda e: e.dma_start(out=dvt[:], in_=Dv_in), writes=["dvt"], dma=True)
            for k in range(16):
                em.op("sp", lambda e, k=k: e.dma_start(out=stage3[:], in_=WoT[k * 128:(k + 1) * 128, :]),
                      writes=["stage3"], dma=True)
                em.op("dve", lambda e, k=k: e.tensor_copy(wob[k][:], stage3[:]),
                      reads=["stage3"], writes=[f"wob{k}"])

            def p3load(dk):
                r0 = dk * 128
                pz = dk % 2
                em.op("sp", lambda e, r0=r0, pz=pz: e.dma_start(out=ya[pz][:], in_=d_y[r0:r0 + 128, 0:WTOK]),
                      writes=[f"ya{pz}"], dma=True)
                em.op("pool", lambda e, r0=r0, pz=pz: e.dma_start(out=yb[pz][:], in_=d_y[r0:r0 + 128, WOFF:WOFF + WTOK]),
                      writes=[f"yb{pz}"], dma=True)
                em.op("sp", lambda e, r0=r0, pz=pz: e.dma_start(out=za[pz][:], in_=d_zs[r0:r0 + 128, 0:WTOK]),
                      writes=[f"za{pz}"], dma=True)
                em.op("pool", lambda e, r0=r0, pz=pz: e.dma_start(out=zb[pz][:], in_=d_zs[r0:r0 + 128, WOFF:WOFF + WTOK]),
                      writes=[f"zb{pz}"], dma=True)
                em.op("sp", lambda e, r0=r0, pz=pz: e.dma_start(out=xa[pz][:], in_=d_xc[r0:r0 + 128, 0:WTOK]),
                      writes=[f"xa{pz}"], dma=True)
                em.op("pool", lambda e, r0=r0, pz=pz: e.dma_start(out=xb2[pz][:], in_=d_xc[r0:r0 + 128, WOFF:WOFF + WTOK]),
                      writes=[f"xb2_{pz}"], dma=True)
            p3load(0)
            for dk in range(16):
                pz = dk % 2
                r0 = dk * 128
                if dk + 1 < 16:
                    p3load(dk + 1)
                blend("dve", ybl[:], ya[pz][:], yb[pz][:], [f"ya{pz}", f"yb{pz}"], "ybl")
                blend("dve", zbl[:], za[pz][:], zb[pz][:], [f"za{pz}", f"zb{pz}"], "zbl")
                blend("dve", xbl[:], xa[pz][:], xb2[pz][:], [f"xa{pz}", f"xb2_{pz}"], "xbl")
                em.op("dve", lambda e: e.tensor_mul(ybl[:], ybl[:], zbl[:]),
                      reads=["ybl", "zbl"], writes=["ybl"])
                em.op("dve", lambda e, dk=dk: e.scalar_tensor_tensor(
                    yf[:], xbl[:], dvt[:, dk:dk + 1], ybl[:], ALU.mult, ALU.add),
                    reads=["xbl", "ybl"], scalar_reads=["dvt"], writes=["yf"])
                em.op("sp", lambda e, r0=r0: e.dma_start(out=d_yf[r0:r0 + 128, :], in_=yf[:]),
                      reads=["yf"], dma=True)

            em.fence()
            with ExitStack() as pp3:
                pd3 = [pp3.enter_context(nc.psum_tensor(f"pd3{i}", [128, 512], F32)) for i in range(6)]
                NTC3 = WTOK // 128
                for g0 in range(0, NTC3, 3):
                    gn = min(3, NTC3 - g0)
                    for dk in range(16):
                        pi = dk % 2
                        em.op("sp" if dk % 2 == 0 else "pool",
                              lambda e, g0=g0, gn=gn, dk=dk, pi=pi: e.dma_start(
                                  out=yfg[pi][:, 0:gn * 128],
                                  in_=d_yf[dk * 128:(dk + 1) * 128, g0 * 128:(g0 + gn) * 128]),
                              writes=[f"yfg{pi}"], dma=True)
                        for j in range(gn):
                            for half in range(2):
                                em.op("pe", lambda e, dk=dk, pi=pi, j=j, half=half: e.matmul(
                                    pd3[2 * j + half][:],
                                    yfg[pi][:, j * 128:(j + 1) * 128],
                                    wob[dk][:, half * 512:(half + 1) * 512],
                                    start=(dk == 0), stop=(dk == 15)),
                                    reads=[f"yfg{pi}", f"wob{dk}"],
                                    writes=[f"pd3{2 * j + half}"] if dk in (0, 15) else [])
                    for j in range(gn):
                        trow = (g0 + j) * 128
                        em.op("sp", lambda e, trow=trow: e.dma_start(
                            out=stage3[:], in_=xres_win[trow:trow + 128, :]),
                            writes=["stage3"], dma=True)
                        for half in range(2):
                            em.op("dve", lambda e, j=j, half=half: e.tensor_add(
                                xr3[:, half * 512:(half + 1) * 512], pd3[2 * j + half][:],
                                stage3[:, half * 512:(half + 1) * 512]),
                                reads=[f"pd3{2 * j + half}", "stage3"],
                                writes=[("xr3", half)])
                        emit_ln(em, nc, xr3, [("xr3", 0), ("xr3", 1)], sq3, "sq3", mv3, epsb, "a3")
                        em.op("sp", lambda e, trow=trow: e.dma_start(
                            out=d_x1[trow:trow + 128, :], in_=xr3[:]),
                            reads=[("xr3", 0), ("xr3", 1)], dma=True)

        em.fence()
        with ExitStack() as p4:
            s4 = lambda n, sh, dt: p4.enter_context(nc.sbuf_tensor(n, sh, dt))
            stage = s4("stageB", [128, 2048], F32)
            HH = DH_B // 2
            FW = 2048
            tA = s4("tAB", [128, FW], F32)
            tB = s4("tBB", [128, FW], F32)
            tI = s4("tIB", [128, FW], I32)
            bm = s4("bm", [128, FW // 16], F32)
            r12 = s4("r12", [128, FW // 16], F32)
            s12 = s4("s12", [128, FW // 16], F32)
            gsc = s4("gsc", [128, FW], BF16)
            scr = (tA, tB, tI, bm, r12, s12, gsc)
            wq = s4("wqB", [128, DH_B], BF16)
            xqT = [s4(f"xqT{k}", [128, 128], BF16) for k in range(8)]
            wgT = [s4(f"wgT{k}", [128, DH_B], BF16) for k in range(8)]
            wuT = [s4(f"wuT{k}", [128, DH_B], BF16) for k in range(8)]
            wdT = [s4(f"wdT{j}", [128, D_MODEL], BF16) for j in range(22)]
            ht = s4("htB", [128, DH_B], F32)
            hq = s4("hqB", [128, DH_B], BF16)
            hqT = [s4(f"hqT{j}", [128, 128], BF16) for j in range(22)]
            xrB = s4("xrB", [128, D_MODEL], F32)
            mvB = s4("mvB", [128, 4], F32)

            with ExitStack() as pp4:
                ptr = [pp4.enter_context(nc.psum_tensor(f"ptrB{i}", [128, 128], BF16)) for i in range(2)]
                for wsrc, wdstT, nm in [(wg, wgT, "wgT"), (wu, wuT, "wuT")]:
                    for c in range(0, 22, 2):
                        for cc in range(2):
                            em.op("sp" if cc == 0 else "pool",
                                  lambda e, c=c, cc=cc, wsrc=wsrc: e.dma_start(
                                      out=stage[:, cc * 1024:(cc + 1) * 1024],
                                      in_=wsrc[(c + cc) * 128:(c + cc + 1) * 128, :]),
                                  writes=["stage"], dma=True)
                        emit_fp4(em, nc, stage[:, 0:FW], "stage", wq[:, 0:FW], "wq", FW, scr)
                        for cc in range(2):
                            for k in range(8):
                                pi = k % 2
                                em.op("pe", lambda e, k=k, cc=cc, pi=pi: e.transpose(
                                    ptr[pi][:], wq[:, cc * 1024 + k * 128:cc * 1024 + (k + 1) * 128], idb[:]),
                                    reads=["wq", "idb"], writes=[f"ptrB{pi}"])
                                em.op("act" if k % 2 == 0 else "dve",
                                      lambda e, c=c, cc=cc, k=k, pi=pi, wdstT=wdstT, nm=nm: _cp(
                                          e, wdstT[k][:, (c + cc) * 128:(c + cc + 1) * 128], ptr[pi][:]),
                                      reads=[f"ptrB{pi}"], writes=[nm + str(k)])
                for c in range(8):
                    em.op("sp", lambda e, c=c: e.dma_start(
                        out=stage[:, 0:FW], in_=wdn[c * 128:(c + 1) * 128, 0:FW]),
                        writes=["stage"], dma=True)
                    emit_fp4(em, nc, stage[:, 0:FW], "stage", wq[:, 0:FW], "wq0", FW, scr)
                    em.op("sp", lambda e, c=c: e.dma_start(
                        out=stage[:, 0:DH_B - FW], in_=wdn[c * 128:(c + 1) * 128, FW:DH_B]),
                        writes=["stage"], dma=True)
                    emit_fp4(em, nc, stage[:, 0:DH_B - FW], "stage", wq[:, FW:DH_B], "wq", DH_B - FW, scr)
                    for j in range(22):
                        pi = j % 2
                        em.op("pe", lambda e, j=j, pi=pi: e.transpose(
                            ptr[pi][:], wq[:, j * 128:(j + 1) * 128], idb[:]),
                            reads=["wq", "wq0", "idb"], writes=[f"ptrB{pi}"])
                        em.op("act" if j % 2 == 0 else "dve",
                              lambda e, c=c, j=j, pi=pi: _cp(
                                  e, wdT[j][:, c * 128:(c + 1) * 128], ptr[pi][:]),
                              reads=[f"ptrB{pi}"], writes=[f"wdT{j}"])

            em.fence()
            with ExitStack() as pp4b:
                pg = pp4b.enter_context(nc.psum_tensor("pg", [128, 512], F32))
                pu = pp4b.enter_context(nc.psum_tensor("pu", [128, 512], F32))
                pd = [pp4b.enter_context(nc.psum_tensor(f"pdB{i}", [128, 512], F32)) for i in range(2)]
                ptr2 = [pp4b.enter_context(nc.psum_tensor(f"ptr2B{i}", [128, 128], BF16)) for i in range(2)]
                GF = [(i * 512, min(512, DH_B - i * 512)) for i in range((DH_B + 511) // 512)]
                NTCB = WTOK // 128
                em.op("sp", lambda e: e.dma_start(
                    out=stage[:, 0:D_MODEL], in_=d_x1[0:128, :]),
                    writes=["stage"], dma=True)
                def xquant(tcq):
                    emit_fp4(em, nc, stage[:, 0:D_MODEL], "stage", wq[:, 0:D_MODEL], "wq", D_MODEL, scr)
                    if tcq + 1 < NTCB:
                        em.op("sp", lambda e, tcq=tcq: e.dma_start(
                            out=stage[:, 0:D_MODEL], in_=d_x1[(tcq + 1) * 128:(tcq + 2) * 128, :]),
                            writes=["stage"], dma=True)
                    for k in range(8):
                        pi = k % 2
                        em.op("pe", lambda e, k=k, pi=pi: e.transpose(
                            ptr2[pi][:], wq[:, k * 128:(k + 1) * 128], idb[:]),
                            reads=["wq", "idb"], writes=[f"ptr2B{pi}"])
                        em.op("act", lambda e, k=k, pi=pi: e.copy(xqT[k][:], ptr2[pi][:]),
                              reads=[f"ptr2B{pi}"], writes=[f"xqT{k}"])
                xquant(0)
                for tc in range(NTCB):
                    for (g0, gl) in GF:
                        for k in range(8):
                            em.op("pe", lambda e, g0=g0, gl=gl, k=k: e.matmul(
                                pg[:, 0:gl], xqT[k][:], wgT[k][:, g0:g0 + gl],
                                start=(k == 0), stop=(k == 7)),
                                reads=[f"xqT{k}", f"wgT{k}"], writes=["pg"] if k in (0, 7) else [])
                            em.op("pe", lambda e, g0=g0, gl=gl, k=k: e.matmul(
                                pu[:, 0:gl], xqT[k][:], wuT[k][:, g0:g0 + gl],
                                start=(k == 0), stop=(k == 7)),
                                reads=[f"xqT{k}", f"wuT{k}"], writes=["pu"] if k in (0, 7) else [])
                        em.op("act", lambda e, g0=g0, gl=gl: e.activation(
                            ht[:, g0:g0 + gl], pg[:, 0:gl], AF.Silu),
                            reads=["pg"], writes=[("ht", g0)])
                        em.op("dve", lambda e, g0=g0, gl=gl: e.tensor_tensor(
                            ht[:, g0:g0 + gl], ht[:, g0:g0 + gl], pu[:, 0:gl], ALU.mult),
                            reads=[("ht", g0), "pu"], writes=[("ht", g0)])
                    if tc + 1 < NTCB:
                        xquant(tc + 1)
                    htkeys = [("ht", g0) for (g0, gl) in GF]
                    emit_fp4(em, nc, ht[:, 0:FW], htkeys, hq[:, 0:FW], "hq0", FW, scr)
                    emit_fp4(em, nc, ht[:, FW:DH_B], htkeys, hq[:, FW:DH_B], "hq", DH_B - FW, scr)
                    for j in range(22):
                        pi = j % 2
                        em.op("pe", lambda e, j=j, pi=pi: e.transpose(
                            ptr2[pi][:], hq[:, j * 128:(j + 1) * 128], idb[:]),
                            reads=["hq", "hq0", "idb"], writes=[f"ptr2B{pi}"])
                        em.op("act", lambda e, j=j, pi=pi: e.copy(hqT[j][:], ptr2[pi][:]),
                              reads=[f"ptr2B{pi}"], writes=[f"hqT{j}"])
                    for half in range(2):
                        for j in range(22):
                            em.op("pe", lambda e, half=half, j=j: e.matmul(
                                pd[half][:], hqT[j][:], wdT[j][:, half * 512:(half + 1) * 512],
                                start=(j == 0), stop=(j == 21)),
                                reads=[f"hqT{j}", f"wdT{j}"], writes=[f"pdB{half}"] if j in (0, 21) else [])
                    em.op("pool", lambda e, tc=tc: e.dma_start(
                        out=tB[:, 0:D_MODEL], in_=d_x1[tc * 128:(tc + 1) * 128, :]),
                        writes=["q_tB"], dma=True)
                    for half in range(2):
                        em.op("dve", lambda e, half=half: e.tensor_add(
                            xrB[:, half * 512:(half + 1) * 512], pd[half][:],
                            tB[:, half * 512:(half + 1) * 512]),
                            reads=[f"pdB{half}", "q_tB"], writes=[("xrB", half)])
                    emit_ln(em, nc, xrB, [("xrB", 0), ("xrB", 1)], tA, "q_tA", mvB, epsb, "b")
                    em.op("sp", lambda e, tc=tc: e.dma_start(
                        out=d_x2[tc * 128:(tc + 1) * 128, :], in_=xrB[:]),
                        reads=[("xrB", 0), ("xrB", 1)], dma=True)

        em.fence()
        with ExitStack() as p4b:
            s4b = lambda n, sh, dt: p4b.enter_context(nc.sbuf_tensor(n, sh, dt))
            ra = s4b("ra", [128, D_MODEL], F32)
            rb = s4b("rb", [128, D_MODEL], F32)
            ro = s4b("ro", [128, D_MODEL], F32)
            for ct in range(COUT // 128):
                em.op("sp", lambda e, ct=ct: e.dma_start(
                    out=ra[:], in_=d_x2[ct * 128:(ct + 1) * 128, :]), writes=["ra"], dma=True)
                em.op("pool", lambda e, ct=ct: e.dma_start(
                    out=rb[:], in_=d_x2[WOFF + ct * 128:WOFF + (ct + 1) * 128, :]), writes=["rb"], dma=True)
                blend("dve", ro[:], ra[:], rb[:], ["ra", "rb"], "ro")
                em.op("sp", lambda e, ct=ct: e.dma_start(
                    out=d_x2res[ct * 128:(ct + 1) * 128, :], in_=ro[:]), reads=["ro"], dma=True)

        em.fence()
        W = 512
        with ExitStack() as p5:
            s5 = lambda n, sh, dt: p5.enter_context(nc.sbuf_tensor(n, sh, dt))
            stage5 = s5("stage5", [128, 3 * D_MODEL], F32)
            wqkvb = [s5(f"wqkvb{k}", [128, 3 * D_MODEL], BF16) for k in range(8)]
            woutb = [s5(f"woutb{k}", [128, D_MODEL], BF16) for k in range(8)]
            bqt = s5("bqt", [128, 24], F32)
            boutb = s5("boutb", [128, D_MODEL], F32)
            brow = s5("brow", [1, D_MODEL], F32)
            xwb = [s5(f"xwb{k}", [128, W], BF16) for k in range(8)]
            xtk = [s5(f"xtk{i}", [128, D_MODEL], BF16) for i in range(2)]
            qkvt = [s5(f"qkvt{m}", [128, W], BF16) for m in range(24)]
            exps = s5("exps", [128, W], F32)
            smax = s5("smax", [128, 4], F32)
            probsT = [s5(f"probsT{kc}", [128, W], BF16) for kc in range(4)]
            probs = s5("probs", [128, W], BF16)
            vT = [s5(f"vT{kc}", [128, 64], BF16) for kc in range(4)]
            owt = [s5(f"owt{k}", [128, W], BF16) for k in range(8)]
            xrC = s5("xrC", [128, D_MODEL], F32)
            xfT = [s5(f"xfT{i}", [128, 128], BF16) for i in range(2)]
            xrCb = s5("xrCb", [128, D_MODEL], BF16)
            sqC = s5("sqC", [128, D_MODEL], F32)
            ones1 = s5("ones1C", [1, 128], F32)
            mvC = s5("mvC", [128, 4], F32)

            with ExitStack() as pp5:
                pq = [pp5.enter_context(nc.psum_tensor(f"pq{i}", [128, W], F32)) for i in range(2)]
                pscore = pp5.enter_context(nc.psum_tensor("pscore", [128, W], F32))
                ptr5 = [pp5.enter_context(nc.psum_tensor(f"ptr5{i}", [128, 128], BF16)) for i in range(2)]
                pouth = pp5.enter_context(nc.psum_tensor("pouth", [64, W], F32))
                pop = [pp5.enter_context(nc.psum_tensor(f"pop{i}", [128, 512], F32)) for i in range(2)]

                em.op("pool", lambda e: e.memset(ones1[:], 1.0), writes=["ones1"])
                em.op("sp", lambda e: e.dma_start(out=bqt[:], in_=bqkv), writes=["bqt"], dma=True)
                em.op("sp", lambda e: e.dma_start(out=brow[:], in_=bout), writes=["brow"], dma=True)
                for k in range(8):
                    em.op("sp", lambda e, k=k: e.dma_start(out=stage5[:], in_=wqkv[k * 128:(k + 1) * 128, :]),
                          writes=["stage5"], dma=True)
                    em.op("dve", lambda e, k=k: e.tensor_copy(wqkvb[k][:], stage5[:]),
                          reads=["stage5"], writes=[f"wqkvb{k}"])
                for k in range(8):
                    em.op("sp", lambda e, k=k: e.dma_start(out=stage5[:, 0:D_MODEL], in_=wout[k * 128:(k + 1) * 128, :]),
                          writes=["stage5"], dma=True)
                    em.op("dve", lambda e, k=k: e.tensor_copy(woutb[k][:], stage5[:, 0:D_MODEL]),
                          reads=["stage5"], writes=[f"woutb{k}"])
                em.op("pe", lambda e: e.matmul(pop[0][:, 0:512], ones1[:], brow[:, 0:512],
                                               start=True, stop=True),
                      reads=["ones1", "brow"], writes=["pop0"])
                em.op("act", lambda e: e.copy(boutb[:, 0:512], pop[0][:, 0:512]), reads=["pop0"], writes=["boutb"])
                em.op("pe", lambda e: e.matmul(pop[0][:, 0:512], ones1[:], brow[:, 512:1024],
                                               start=True, stop=True),
                      reads=["ones1", "brow"], writes=["pop0"])
                em.op("act", lambda e: e.copy(boutb[:, 512:1024], pop[0][:, 0:512]), reads=["pop0"], writes=["boutb"])

                for w in range(NWIN):
                    for j in range(4):
                        jz = j % 2
                        em.op("sp", lambda e, w=w, j=j: e.dma_start(
                            out=stage5[:, 0:D_MODEL],
                            in_=d_x2[256 * w + 128 * j:256 * w + 128 * (j + 1), :]),
                            writes=["stage5"], dma=True)
                        em.op("dve", lambda e, jz=jz: e.tensor_copy(xtk[jz][:], stage5[:, 0:D_MODEL]),
                              reads=["stage5"], writes=[f"xtk{jz}"])
                        for kc in range(8):
                            pi = kc % 2
                            em.op("pe", lambda e, kc=kc, pi=pi, jz=jz: e.transpose(
                                ptr5[pi][:], xtk[jz][:, kc * 128:(kc + 1) * 128], idb[:]),
                                reads=[f"xtk{jz}", "idb"], writes=[f"ptr5{pi}"])
                            em.op("act" if kc % 2 == 0 else "dve",
                                  lambda e, kc=kc, j=j, pi=pi: _cp(e, 
                                      xwb[kc][:, j * 128:(j + 1) * 128], ptr5[pi][:]),
                                  reads=[f"ptr5{pi}"], writes=[f"xwb{kc}"])
                    for m in range(24):
                        pt = pq[m % 2]
                        pkey = f"pq{m % 2}"
                        for k in range(8):
                            em.op("pe", lambda e, m=m, k=k, pt=pt: e.matmul(
                                pt[:], wqkvb[k][:, m * 128:(m + 1) * 128], xwb[k][:],
                                start=(k == 0), stop=(k == 7)),
                                reads=[f"wqkvb{k}", f"xwb{k}"], writes=[pkey] if k in (0, 7) else [])
                        em.op("act", lambda e, m=m, pt=pt: e.activation(
                            qkvt[m][:], pt[:], AF.Identity, bias=bqt[:, m:m + 1]),
                            reads=[pkey], scalar_reads=["bqt"], writes=[f"qkvt{m}"])
                    for h in range(16):
                        qc_t = qkvt[h // 2]
                        kc_t = qkvt[8 + h // 2]
                        vc_t = qkvt[16 + h // 2]
                        off = (h % 2) * 64
                        qkey, kkey, vkey = f"qkvt{h // 2}", f"qkvt{8 + h // 2}", f"qkvt{16 + h // 2}"
                        for kc in range(4):
                            pi = kc % 2
                            em.op("pe", lambda e, kc=kc, vc_t=vc_t, off=off, pi=pi: e.transpose(
                                ptr5[pi][:, 0:64], vc_t[off:off + 64, kc * 128:(kc + 1) * 128],
                                idb[off:off + 64, off:off + 64]),
                                reads=[vkey, "idb"], writes=[f"ptr5{pi}"])
                            em.op("act" if kc % 2 == 0 else "dve",
                                  lambda e, kc=kc, pi=pi: _cp(e, vT[kc][:], ptr5[pi][:, 0:64]),
                                  reads=[f"ptr5{pi}"], writes=[f"vT{kc}"])
                        for qc in range(4):
                            em.op("pe", lambda e, qc=qc, qc_t=qc_t, kc_t=kc_t, off=off: e.matmul(
                                pscore[:], qc_t[off:off + 64, qc * 128:(qc + 1) * 128], kc_t[off:off + 64, :],
                                start=True, stop=True),
                                reads=[qkey, kkey], writes=["pscore"])
                            em.op("dve", lambda e: e.tensor_reduce(
                                smax[:, 0:1], pscore[:], AX.X, ALU.max),
                                reads=["pscore"], writes=["smax"])
                            em.op("dve", lambda e: e.tensor_scalar_mul(smax[:, 1:2], smax[:, 0:1], -0.125),
                                  reads=["smax"], writes=["smax1"])
                            em.op("act", lambda e: e.activation(
                                exps[:], pscore[:], AF.Exp, bias=smax[:, 1:2], scale=0.125,
                                accum_out=smax[:, 2:3]),
                                reads=["pscore"], scalar_reads=["smax1"], writes=["exps", "smax2"])
                            em.op("dve", lambda e: e.reciprocal(smax[:, 3:4], smax[:, 2:3]),
                                  reads=["smax2"], writes=["smax3"])
                            em.op("dve", lambda e: e.tensor_scalar(
                                probs[:], exps[:], smax[:, 3:4], None, ALU.mult),
                                reads=["exps"], scalar_reads=["smax3"], writes=["probs"])
                            for kc in range(4):
                                pi = kc % 2
                                em.op("pe", lambda e, kc=kc, pi=pi: e.transpose(
                                    ptr5[pi][:], probs[:, kc * 128:(kc + 1) * 128], idb[:]),
                                    reads=["probs", "idb"], writes=[f"ptr5{pi}"])
                                em.op("act" if kc % 2 == 0 else "dve",
                                      lambda e, kc=kc, qc=qc, pi=pi: _cp(e, 
                                          probsT[kc][:, qc * 128:(qc + 1) * 128], ptr5[pi][:]),
                                      reads=[f"ptr5{pi}"], writes=[f"probsT{kc}"])
                        for kc in range(4):
                            em.op("pe", lambda e, kc=kc: e.matmul(
                                pouth[:], vT[kc][:], probsT[kc][:],
                                start=(kc == 0), stop=(kc == 3)),
                                reads=[f"vT{kc}", f"probsT{kc}"], writes=["pouth"] if kc in (0, 3) else [])
                        em.op("act", lambda e, h=h: e.copy(owt[h // 2][(h % 2) * 64:(h % 2) * 64 + 64, :], pouth[:]),
                              reads=["pouth"], writes=[(f"owt{h // 2}", h % 2)])
                    for ts4 in range(4):
                        for half in range(2):
                            for kc in range(8):
                                em.op("pe", lambda e, ts4=ts4, half=half, kc=kc: e.matmul(
                                    pop[half][:], owt[kc][:, ts4 * 128:(ts4 + 1) * 128],
                                    woutb[kc][:, half * 512:(half + 1) * 512],
                                    start=(kc == 0), stop=(kc == 7)),
                                    reads=[(f"owt{kc}", 0), (f"owt{kc}", 1), f"woutb{kc}"],
                                    writes=[f"pop{half}"] if kc in (0, 7) else [])
                        trow = w * 512 + ts4 * 128
                        em.op("sp", lambda e, trow=trow: e.dma_start(
                            out=stage5[:, 0:D_MODEL], in_=d_x2res[trow:trow + 128, :]),
                            writes=["stage5"], dma=True)
                        for half in range(2):
                            em.op("dve", lambda e, half=half: e.tensor_add(
                                xrC[:, half * 512:(half + 1) * 512], pop[half][:],
                                stage5[:, half * 512:(half + 1) * 512]),
                                reads=[f"pop{half}", "stage5"], writes=[("xrC", half)])
                        em.op("dve", lambda e: e.tensor_add(xrC[:], xrC[:], boutb[:]),
                              reads=[("xrC", 0), ("xrC", 1), "boutb"], writes=["xrC"])
                        emit_ln(em, nc, xrC, ["xrC"], sqC, "sqC", mvC, epsb, "c")
                        em.op("sp", lambda e, trow=trow: e.dma_start(
                            out=d_x3tok[trow:trow + 128, :], in_=xrC[:]),
                            reads=["xrC"], dma=True)
                        em.op("dve", lambda e: e.tensor_copy(xrCb[:], xrC[:]),
                              reads=["xrC"], writes=["xrCb"])
                        for kc in range(8):
                            pi = kc % 2
                            em.op("pe", lambda e, kc=kc, pi=pi: e.transpose(
                                ptr5[pi][:], xrCb[:, kc * 128:(kc + 1) * 128], idb[:]),
                                reads=["xrCb", "idb"], writes=[f"ptr5{pi}"])
                            em.op("act" if kc % 2 == 0 else "dve",
                                  lambda e, kc=kc, pi=pi: _cp(e, xfT[kc % 2][:], ptr5[pi][:]),
                                  reads=[f"ptr5{pi}"], writes=[f"xfT{kc % 2}"])
                            em.op("pool", lambda e, kc=kc, trow=trow: e.dma_start(
                                out=d_x3T[kc * 128:(kc + 1) * 128, 1 + trow:1 + trow + 128],
                                in_=xfT[kc % 2][:]),
                                reads=[f"xfT{kc % 2}"], dma=True)

        em.fence()
        with ExitStack() as p5b:
            s5b = lambda n, sh, dt: p5b.enter_context(nc.sbuf_tensor(n, sh, dt))
            ha = s5b("ha", [128, DOUT + 2], F32)
            hb = s5b("hb", [128, DOUT + 2], F32)
            ho = s5b("ho", [128, DOUT + 2], F32)
            hab = s5b("hab", [128, DOUT + 2], BF16)
            hbb = s5b("hbb", [128, DOUT + 2], BF16)
            hob = s5b("hob", [128, DOUT + 2], BF16)
            for k in range(8):
                em.op("sp", lambda e, k=k: e.dma_start(
                    out=hab[:], in_=d_x3T[k * 128:(k + 1) * 128, 0:DOUT + 2]), writes=["hab"], dma=True)
                em.op("pool", lambda e, k=k: e.dma_start(
                    out=hbb[:], in_=d_x3T[k * 128:(k + 1) * 128, 512:512 + DOUT + 2]), writes=["hbb"], dma=True)
                blend("dve", hob[:], hab[:], hbb[:], ["hab", "hbb"], "hob")
                em.op("sp", lambda e, k=k: e.dma_start(
                    out=d_x3w[k * 128:(k + 1) * 128, :], in_=hob[:]), reads=["hob"], dma=True)
            for ct in range(DOUT // 128):
                em.op("sp", lambda e, ct=ct: e.dma_start(
                    out=ha[:, 0:D_MODEL], in_=d_x3tok[ct * 128:(ct + 1) * 128, :]), writes=["ha"], dma=True)
                em.op("pool", lambda e, ct=ct: e.dma_start(
                    out=hb[:, 0:D_MODEL], in_=d_x3tok[512 + ct * 128:512 + (ct + 1) * 128, :]), writes=["hb"], dma=True)
                blend("dve", ho[:, 0:D_MODEL], ha[:, 0:D_MODEL], hb[:, 0:D_MODEL], ["ha", "hb"], "ho")
                em.op("sp", lambda e, ct=ct: e.dma_start(
                    out=d_x3res[ct * 128:(ct + 1) * 128, :], in_=ho[:, 0:D_MODEL]), reads=["ho"], dma=True)

        em.fence()
        TH_D = DOUT + 2
        with ExitStack() as p6:
            s6 = lambda n, sh, dt: p6.enter_context(nc.sbuf_tensor(n, sh, dt))
            TPAD = 2080
            NT32 = TPAD // 32
            w1s = [s6(f"w1s{k}", [128, DH_D], BF16) for k in range(8)]
            w2s = [s6(f"w2s{k}", [128, D_MODEL], BF16) for k in range(32)]
            stage = s6("stageD", [128, 2052], F32)
            scr4 = s6("scr4", [128, 2048], F32)
            scaleb = s6("scaleb", [128, TPAD], BF16)
            invb = s6("invb", [128, TPAD], BF16)
            m32t = s6("m32t", [32, TPAD], BF16)
            tr32 = s6("tr32", [32, 1056], BF16)
            cmt = s6("cmt", [32, 2 * NT32], F32)
            cmb = s6("cmb", [32, 144], BF16)
            selq = s6("selq", [32, 32 * 128], BF16)
            xqi = [s6(f"xqi{k}", [128, 258], BF16) for k in range(8)]
            xld = [s6(f"xld{i}", [128, 258], BF16) for i in range(2)]
            x3b = s6("x3b", [128, 2052], BF16)
            acc1 = s6("acc1", [128, 2], F32)
            sc01 = s6("sc01", [1, 4], F32)
            m1b = s6("m1b", [128, 2], F32)
            hmul = s6("hmul", [128, 2], F32)
            invT = s6("invT", [128, 2], F32)
            cwt6 = s6("cwt6", [128, 96], F32)
            cbt6 = s6("cbt6", [128, 32], F32)
            ht6 = s6("ht6", [128, 258], F32)
            h2q = [s6(f"h2q{m}", [128, 256], BF16) for m in range(32)]
            habs = s6("habs", [128, 256], F32)
            xrD = s6("xrD", [128, D_MODEL], F32)
            mvD = s6("mvD", [128, 4], F32)
            ones1 = s6("ones1D", [1, 128], F32)
            onesc = s6("onescD", [128, 1], F32)

            em.op("pool", lambda e: e.memset(ones1[:], 1.0), writes=["ones1"])
            em.op("pool", lambda e: e.memset(onesc[:], 1.0), writes=["onesc"])
            em.op("sp", lambda e: e.dma_start(out=cwt6[:], in_=cw1), writes=["cwt6"], dma=True)
            em.op("sp", lambda e: e.dma_start(out=cbt6[:], in_=cb1), writes=["cbt6"], dma=True)
            em.op("pool", lambda e: e.memset(selq[:], 1.0), writes=["selq"])
            for n in range(32):
                em.op("pool", lambda e, n=n: e.affine_select(
                    out=selq[:, n * 128:(n + 1) * 128], in_=selq[:, n * 128:(n + 1) * 128],
                    compare_op=ALU.is_equal, fill=0.0, base=-n, pattern=[[0, 128]],
                    channel_multiplier=1), reads=["selq"], writes=["selq"] if n == 31 else [])

            with ExitStack() as pp6:
                pp = pp6.enter_context(nc.psum_tensor("ppD", [128, 512], F32))
                pqD = pp6.enter_context(nc.psum_tensor("pqD", [1, 512], F32))
                for wsrc, nchunk, wcols, acol in [(w1T, 8, DH_D, 0), (w2T, 32, D_MODEL, 1)]:
                    first = True
                    for c in range(nchunk):
                        for h0 in range(0, wcols, 2048):
                            cl = min(2048, wcols - h0)
                            em.op("sp", lambda e, c=c, h0=h0, cl=cl, wsrc=wsrc: e.dma_start(
                                out=stage[:, 0:cl], in_=wsrc[c * 128:(c + 1) * 128, h0:h0 + cl]),
                                writes=["stage"], dma=True)
                            if first:
                                em.op("dve", lambda e, acol=acol, cl=cl: e.tensor_reduce(
                                    acc1[:, acol:acol + 1], stage[:, 0:cl], AX.X, ALU.add,
                                    apply_absolute_value=True),
                                    reads=["stage"], writes=[f"acc{acol}"])
                                first = False
                            else:
                                em.op("dve", lambda e, cl=cl: e.tensor_reduce(
                                    mvD[:, 0:1], stage[:, 0:cl], AX.X, ALU.add,
                                    apply_absolute_value=True),
                                    reads=["stage"], writes=["mtmp"])
                                em.op("dve", lambda e, acol=acol: e.tensor_add(
                                    acc1[:, acol:acol + 1], acc1[:, acol:acol + 1], mvD[:, 0:1]),
                                    reads=["mtmp", f"acc{acol}"], writes=[f"acc{acol}"])
                em.op("pe", lambda e: e.matmul(pqD[:, 0:2], onesc[:], acc1[:, 0:2], start=True, stop=True),
                      reads=["acc0", "acc1", "onesc"], writes=["pqD"])
                em.op("act", lambda e: e.copy(sc01[:, 0:2], pqD[:, 0:2]), reads=["pqD"], writes=["sc01"])
                em.op("dve", lambda e: e.tensor_scalar_mul(sc01[:, 0:2], sc01[:, 0:2], 1.0 / (D_MODEL * DH_D)),
                      reads=["sc01"], writes=["sc01"])
                em.op("dve", lambda e: e.tensor_scalar_max(sc01[:, 0:2], sc01[:, 0:2], 1e-5),
                      reads=["sc01"], writes=["sc01"])
                em.op("pe", lambda e: e.matmul(pp[:, 0:2], ones1[:], sc01[:, 0:2], start=True, stop=True),
                      reads=["sc01", "ones1"], writes=["ppD"])
                em.op("act", lambda e: e.copy(m1b[:, 0:2], pp[:, 0:2]), reads=["ppD"], writes=["m1b"])
                em.op("dve", lambda e: e.tensor_scalar_mul(hmul[:, 0:2], m1b[:, 0:2], 0.5),
                      reads=["m1b"], writes=["hmul"])

                for wsrc, dst, nchunk, wcols, acol, nm in [(w1T, w1s, 8, DH_D, 0, "w1s"),
                                                           (w2T, w2s, 32, D_MODEL, 1, "w2s")]:
                    for c in range(nchunk):
                        for h0 in range(0, wcols, 2048):
                            cl = min(2048, wcols - h0)
                            em.op("sp", lambda e, c=c, h0=h0, cl=cl, wsrc=wsrc: e.dma_start(
                                out=stage[:, 0:cl], in_=wsrc[c * 128:(c + 1) * 128, h0:h0 + cl]),
                                writes=["stage"], dma=True)
                            em.op("act", lambda e, cl=cl: e.activation(scr4[:, 0:cl], stage[:, 0:cl], AF.Abs),
                                  reads=["stage"], writes=["scr4"])
                            em.op("dve", lambda e, cl=cl, acol=acol: e.tensor_scalar(
                                scr4[:, 0:cl], scr4[:, 0:cl], hmul[:, acol:acol + 1], None, ALU.is_gt),
                                reads=["scr4"], scalar_reads=["hmul"], writes=["scr4"])
                            em.op("act", lambda e, cl=cl: e.activation(invb[:, 0:cl], stage[:, 0:cl], AF.Sign),
                                  reads=["stage"], writes=["invb"])
                            em.op("dve", lambda e, c=c, h0=h0, cl=cl, dst=dst, nm=nm: e.tensor_tensor(
                                dst[c][:, h0:h0 + cl], scr4[:, 0:cl], invb[:, 0:cl], ALU.mult),
                                reads=["scr4", "invb"], writes=[nm + str(c)])

                for k in range(8):
                    em.op("sp", lambda e, k=k: e.dma_start(
                        out=x3b[:, 0:TH_D], in_=d_x3w[k * 128:(k + 1) * 128, 0:TH_D]),
                        writes=["x3b"], dma=True)
                    if k == 0:
                        em.op("pool", lambda e: e.memset(scaleb[:], 0.0), writes=["scaleb"])
                        em.op("act", lambda e: e.activation(scaleb[:, 0:TH_D], x3b[:, 0:TH_D], AF.Abs),
                              reads=["x3b"], writes=["scaleb"])
                    else:
                        em.op("act", lambda e: e.activation(invb[:, 0:TH_D], x3b[:, 0:TH_D], AF.Abs),
                              reads=["x3b"], writes=["invb"])
                        em.op("dve", lambda e: e.tensor_max(scaleb[:, 0:TH_D], scaleb[:, 0:TH_D], invb[:, 0:TH_D]),
                              reads=["scaleb", "invb"], writes=["scaleb"])
                em.op("pool", lambda e: e.memset(m32t[:], 0.0), writes=["m32t"])
                em.op("dve", lambda e: e.tensor_copy(invb[0:64, :], scaleb[64:128, :]),
                      reads=["scaleb"], writes=["invb"])
                em.op("dve", lambda e: e.tensor_max(scaleb[0:64, :], scaleb[0:64, :], invb[0:64, :]),
                      reads=["scaleb", "invb"], writes=["scaleb"])
                em.op("dve", lambda e: e.tensor_copy(invb[0:32, :], scaleb[32:64, :]),
                      reads=["scaleb"], writes=["invb"])
                em.op("dve", lambda e: e.tensor_max(m32t[:, :], scaleb[0:32, :], invb[0:32, :]),
                      reads=["scaleb", "invb", "m32t"], writes=["m32t"])
                for (h0, hl) in [(0, 1024), (1024, 1056)]:
                    em.op("dve", lambda e, h0=h0, hl=hl: e.transpose(tr32[:, 0:hl], m32t[:, h0:h0 + hl]),
                          reads=["m32t"], writes=["tr32"])
                    em.op("dve", lambda e, h0=h0, hl=hl: e.tensor_reduce(
                        cmt[:, h0 // 32:(h0 + hl) // 32],
                        tr32[:, 0:hl].rearrange("p (k q) -> p k q", q=32), AX.X, ALU.max),
                        reads=["tr32"], writes=["cmt"])
                em.op("dve", lambda e: e.tensor_scalar_max(cmt[:, 0:NT32], cmt[:, 0:NT32], 1e-5),
                      reads=["cmt"], writes=["cmt"])
                em.op("dve", lambda e: e.reciprocal(cmt[:, NT32:2 * NT32], cmt[:, 0:NT32]),
                      reads=["cmt"], writes=["cmt2"])
                em.op("dve", lambda e: e.tensor_scalar_mul(cmt[:, NT32:2 * NT32], cmt[:, NT32:2 * NT32], 127.0),
                      reads=["cmt2"], writes=["cmt2"])
                em.op("dve", lambda e: e.tensor_scalar_mul(cmt[:, 0:NT32], cmt[:, 0:NT32], 1.0 / 127.0),
                      reads=["cmt"], writes=["cmt"])
                em.op("dve", lambda e: e.tensor_copy(cmb[:, 0:2 * NT32], cmt[:, 0:2 * NT32]),
                      reads=["cmt", "cmt2"], writes=["cmb"])
                for n in range(32):
                    em.op("pe", lambda e, n=n: e.matmul(
                        pp[:, 0:2 * NT32], selq[:, n * 128:(n + 1) * 128], cmb[:, 0:2 * NT32],
                        start=True, stop=True),
                        reads=["selq", "cmb"], writes=["ppD"])
                    em.op("act", lambda e, n=n: e.copy(
                        scaleb.rearrange("p (k q) -> p k q", q=32)[:, :, n], pp[:, NT32:2 * NT32]),
                        reads=["ppD"], writes=["scaleb"])
                    em.op("act", lambda e, n=n: e.activation(
                        invb.rearrange("p (k q) -> p k q", q=32)[:, :, n], pp[:, 0:NT32],
                        AF.Identity, scale=m1b[:, 0:1]),
                        reads=["ppD"], scalar_reads=["m1b"], writes=["invb"])

            em.fence()
            with ExitStack() as pp6b:
                ph = pp6b.enter_context(nc.psum_tensor("ph", [128, 258], F32))
                po = [pp6b.enter_context(nc.psum_tensor(f"po{i}", [128, 512], F32)) for i in range(2)]
                pcm = pp6b.enter_context(nc.psum_tensor("pcm", [128, 16], F32))
                for ts8 in range(8):
                    t0 = ts8 * 256
                    for k in range(8):
                        kz = k % 2
                        em.op("sp", lambda e, k=k, t0=t0, kz=kz: e.dma_start(
                            out=xld[kz][:], in_=d_x3w[k * 128:(k + 1) * 128, t0:t0 + 258]),
                            writes=[f"xld{kz}"], dma=True)
                        em.op("dve", lambda e, t0=t0, kz=kz: e.tensor_tensor(
                            scr4.bitcast(I32)[:, 0:258], xld[kz][:], scaleb[:, t0:t0 + 258], ALU.mult),
                            reads=[f"xld{kz}", "scaleb"], writes=["scr4"])
                        em.op("dve", lambda e, k=k: e.tensor_copy(xqi[k][:], scr4.bitcast(I32)[:, 0:258]),
                              reads=["scr4"], writes=[f"xqi{k}"])
                    for m in range(32):
                        for k in range(8):
                            em.op("pe", lambda e, m=m, k=k: e.matmul(
                                ph[:], w1s[k][:, m * 128:(m + 1) * 128], xqi[k][:],
                                start=(k == 0), stop=(k == 7)),
                                reads=[f"w1s{k}", f"xqi{k}"], writes=["ph"] if k in (0, 7) else [])
                        em.op("dve", lambda e, m=m, t0=t0: e.tensor_tensor(
                            ht6[:], ph[:], invb[:, t0:t0 + 258], ALU.mult),
                            reads=["ph", "invb"], writes=["ht6"])
                        em.op("dve", lambda e, m=m: e.tensor_scalar(
                            stage[:, 0:256], ht6[:, 0:256], cwt6[:, m * 3:m * 3 + 1], None, ALU.mult),
                            reads=["ht6"], scalar_reads=["cwt6"], writes=["stage"])
                        for kk in range(1, 3):
                            em.op("dve", lambda e, m=m, kk=kk: e.scalar_tensor_tensor(
                                stage[:, 0:256], ht6[:, kk:kk + 256], cwt6[:, m * 3 + kk:m * 3 + kk + 1],
                                stage[:, 0:256], ALU.mult, ALU.add),
                                reads=["ht6", "stage"], scalar_reads=["cwt6"], writes=["stage"])
                        em.op("act", lambda e, m=m: e.activation(
                            stage[:, 0:256], stage[:, 0:256], AF.Silu, bias=cbt6[:, m:m + 1]),
                            reads=["stage"], scalar_reads=["cbt6"], writes=["stage"])
                        em.op("dve", lambda e, m=m: e.tensor_copy(h2q[m][:], stage[:, 0:256]),
                              reads=["stage"], writes=[f"h2q{m}"])
                        if m == 0:
                            em.op("act", lambda e: e.activation(scr4[:, 0:256], stage[:, 0:256], AF.Abs),
                                  reads=["stage"], writes=["scr4"])
                        else:
                            em.op("act", lambda e: e.activation(habs[:], stage[:, 0:256], AF.Abs),
                                  reads=["stage"], writes=["habs"])
                            em.op("dve", lambda e: e.tensor_max(scr4[:, 0:256], scr4[:, 0:256], habs[:]),
                                  reads=["scr4", "habs"], writes=["scr4"])
                    em.op("dve", lambda e: e.tensor_copy(habs[0:64, :], scr4[64:128, 0:256]),
                          reads=["scr4"], writes=["habs"])
                    em.op("dve", lambda e: e.tensor_max(scr4[0:64, 0:256], scr4[0:64, 0:256], habs[0:64, :]),
                          reads=["scr4", "habs"], writes=["scr4"])
                    em.op("dve", lambda e: e.tensor_copy(habs[0:32, :], scr4[32:64, 0:256]),
                          reads=["scr4"], writes=["habs"])
                    em.op("dve", lambda e: e.tensor_max(m32t[:, 0:256], scr4[0:32, 0:256], habs[0:32, :]),
                          reads=["scr4", "habs", "m32t"], writes=["m32t"])
                    em.op("dve", lambda e: e.transpose(tr32[:, 0:256], m32t[:, 0:256]),
                          reads=["m32t"], writes=["tr32"])
                    em.op("dve", lambda e: e.tensor_reduce(
                        cmt[:, 0:8], tr32[:, 0:256].rearrange("p (k q) -> p k q", q=32), AX.X, ALU.max),
                        reads=["tr32"], writes=["cmt"])
                    em.op("dve", lambda e: e.tensor_scalar_max(cmt[:, 0:8], cmt[:, 0:8], 1e-5),
                          reads=["cmt"], writes=["cmt"])
                    em.op("dve", lambda e: e.reciprocal(cmt[:, 8:16], cmt[:, 0:8]),
                          reads=["cmt"], writes=["cmt2"])
                    em.op("dve", lambda e: e.tensor_scalar_mul(cmt[:, 8:16], cmt[:, 8:16], 127.0),
                          reads=["cmt2"], writes=["cmt2"])
                    em.op("dve", lambda e: e.tensor_scalar_mul(cmt[:, 0:8], cmt[:, 0:8], 1.0 / 127.0),
                          reads=["cmt"], writes=["cmt"])
                    em.op("dve", lambda e: e.tensor_copy(cmb[:, 0:16], cmt[:, 0:16]),
                          reads=["cmt", "cmt2"], writes=["cmb"])
                    for n in range(32):
                        em.op("pe", lambda e, n=n: e.matmul(
                            pcm[:, 0:16], selq[:, n * 128:(n + 1) * 128], cmb[:, 0:16],
                            start=True, stop=True),
                            reads=["selq", "cmb"], writes=["pcm"])
                        em.op("act", lambda e, n=n: e.copy(
                            habs[:].rearrange("p (k q) -> p k q", q=32)[:, :, n], pcm[:, 8:16]),
                            reads=["pcm"], writes=["habs"])
                    for tt in range(2):
                        for k4 in range(4):
                            em.op("pool", lambda e, tt=tt, k4=k4: e.dma_start(
                                out=invT[32 * k4:32 * k4 + 32, tt:tt + 1],
                                in_=cmt[:, 4 * tt + k4:4 * tt + k4 + 1]),
                                reads=["cmt"], writes=["invTr"], dma=True)
                    em.op("dve", lambda e: e.tensor_scalar(
                        invT[:, 0:2], invT[:, 0:2], m1b[:, 1:2], None, ALU.mult),
                        reads=["invTr"], scalar_reads=["m1b"], writes=["invT"])
                    for m in range(32):
                        em.op("dve", lambda e, m=m: e.tensor_tensor(
                            scr4.bitcast(I32)[:, 0:256], h2q[m][:], habs[:], ALU.mult),
                            reads=[f"h2q{m}", "habs"], writes=["scr4"])
                        em.op("dve", lambda e, m=m: e.tensor_copy(h2q[m][:], scr4.bitcast(I32)[:, 0:256]),
                              reads=["scr4"], writes=[f"h2q{m}"])
                    for tt in range(2):
                        for half in range(2):
                            for k in range(32):
                                em.op("pe", lambda e, tt=tt, half=half, k=k: e.matmul(
                                    po[half][:], h2q[k][:, tt * 128:(tt + 1) * 128],
                                    w2s[k][:, half * 512:(half + 1) * 512],
                                    start=(k == 0), stop=(k == 31)),
                                    reads=[f"h2q{k}", f"w2s{k}"],
                                    writes=[f"po{half}"] if k in (0, 31) else [])
                        trow = t0 + tt * 128
                        em.op("sp", lambda e, trow=trow: e.dma_start(
                            out=stage[:, 0:D_MODEL], in_=d_x3res[trow:trow + 128, :]),
                            writes=["stage"], dma=True)
                        for half in range(2):
                            em.op("dve", lambda e, tt=tt, half=half: e.tensor_scalar(
                                xrD[:, half * 512:(half + 1) * 512], po[half][:],
                                invT[:, tt:tt + 1], None, ALU.mult),
                                reads=[f"po{half}"], scalar_reads=["invT"],
                                writes=[("xrD", half)])
                        for half in range(2):
                            em.op("dve", lambda e, half=half: e.tensor_add(
                                xrD[:, half * 512:(half + 1) * 512], xrD[:, half * 512:(half + 1) * 512],
                                stage[:, half * 512:(half + 1) * 512]),
                                reads=[("xrD", half), "stage"], writes=[("xrD", half)])
                        emit_ln(em, nc, xrD, [("xrD", 0), ("xrD", 1)], scr4, "scr4", mvD, epsb, "d")
                        em.op("sp", lambda e, trow=trow: e.dma_start(
                            out=o_out[trow:trow + 128, :], in_=xrD[:]),
                            reads=[("xrD", 0), ("xrD", 1)], dma=True)

        em.drain()
    return nc


_BUILD_CACHE = {}


def _get(name, builder):
    if name not in _BUILD_CACHE:
        _BUILD_CACHE[name] = builder()
    return _BUILD_CACHE[name]


def kernel(**inputs):
    inputs = {k: np.asarray(v) for k, v in inputs.items()}
    x = inputs["L0_attn_x"]
    W_inT = np.ascontiguousarray(inputs["L0_attn_W_in"].T)
    conv_w = inputs["L0_attn_conv_w"].reshape(D_INNER, D_CONV)
    cw = np.ascontiguousarray(conv_w.reshape(16, 128, D_CONV).transpose(1, 0, 2).reshape(128, 64))
    cb = np.ascontiguousarray(inputs["L0_attn_conv_b"].reshape(16, 128).T)
    bd = np.ascontiguousarray(inputs["L0_attn_b_dt"].reshape(16, 128).T)
    W_xT = np.ascontiguousarray(inputs["L0_attn_W_x"].T)
    W_dtT = np.ascontiguousarray(inputs["L0_attn_W_dt"].T)
    alog = np.ascontiguousarray(
        inputs["L0_attn_A_log"].reshape(16, 128, 16).transpose(1, 0, 2).reshape(128, 256))
    Dv = np.ascontiguousarray(inputs["L0_attn_D"].reshape(16, 128).T)
    WoT = np.ascontiguousarray(inputs["L0_attn_W_out"].T)
    bq = np.ascontiguousarray(inputs["L1_attn_b_qkv"].reshape(24, 128).T)
    bo = np.ascontiguousarray(inputs["L1_attn_b_out"].reshape(1, 1024))
    w1T = np.ascontiguousarray(inputs["L1_ffn_w1"].T)
    w2T = np.ascontiguousarray(inputs["L1_ffn_w2"].T)
    cwm = np.ascontiguousarray(inputs["L1_ffn_conv_w"].reshape(4096, 3)
                               .reshape(32, 128, 3).transpose(1, 0, 2).reshape(128, 96))
    cbm = np.ascontiguousarray(inputs["L1_ffn_conv_b"].reshape(32, 128).T)

    in_maps = []
    for c in range(N_CORES):
        b, h = divmod(c, 2)
        xTb = np.zeros((D_MODEL, SEQ + 3), np.float32)
        xTb[:, 3:] = x[b].T
        mskc = np.zeros((128, 2), np.float32)
        mskc[:, 0] = float(h)
        mskc[:, 1] = 1.0 - float(h)
        in_maps.append({
            "xT": np.ascontiguousarray(xTb),
            "xres_win": np.ascontiguousarray(x[b, WOFF * h:WOFF * h + WTOK, :]),
            "msk": mskc,
            "W_inT": W_inT, "conv_w": cw, "conv_b": cb,
            "W_xT": W_xT, "W_dtT": W_dtT, "b_dt": bd,
            "alog": alog, "Dv": Dv, "WoT": WoT,
            "wg": inputs["L0_ffn_w_gate"], "wu": inputs["L0_ffn_w_up"],
            "wdn": inputs["L0_ffn_w_down"],
            "wqkv": inputs["L1_attn_w_qkv"], "bqkv": bq,
            "wout": inputs["L1_attn_w_out"], "bout": bo,
            "w1T": w1T, "cw1": cwm, "cb1": cbm, "w2T": w2T,
        })

    res = run_bass_kernel_spmd(_get("fused", build_fused), in_maps,
                               core_ids=list(range(N_CORES))).results
    out = np.empty((BATCH, SEQ, D_MODEL), np.float32)
    for c in range(N_CORES):
        b, h = divmod(c, 2)
        out[b, DOUT * h:DOUT * (h + 1)] = res[c]["out_tok"]
    return out
